# version 01
# baseline (speedup 1.0000x reference)
import contextlib
import math

import numpy as np

import concourse.bass as bass
import concourse.mybir as mybir
from concourse.bass_utils import run_bass_kernel_spmd

F32 = mybir.dt.float32
BF16 = mybir.dt.bfloat16
AF = mybir.ActivationFunctionType
ALU = mybir.AluOpType
AX = mybir.AxisListType

D = 1024
DK = 128
E = 8
H = 256
NCHUNK = 8
NCORE = 8
QB = 16.0
EPS_LN = 1e-5
SCALE = 1.0 / math.sqrt(DK)
HC = H // 128
DC = D // 128


class _Stub:
    def then_inc(self, *a, **k):
        return self

    def __enter__(self):
        return self

    def __exit__(self, *exc):
        return False


_STUB = _Stub()


class _Eng:

    def __init__(self, h, active):
        self._h = h
        self._active = active

    def __getattr__(self, name):
        if self._active:
            return getattr(self._h, name)
        return lambda *a, **k: _STUB


def _chunks_of_core(c):
    c0 = c % 4
    return (c0, 7 - c0)


def _pieces(width, maxw=512):
    return [(o, min(maxw, width - o)) for o in range(0, width, maxw)]


def build_nc(T=8192):
    CHUNK = T // NCHUNK
    TPC = 2 * CHUNK
    NTILE = TPC // 128
    KPC = CHUNK // 128
    KPG = T // 128
    W = CHUNK
    HALF = CHUNK
    NTILE_H = HALF // 128
    KOFF = 0
    VOFF = TPC * DK
    BLK = 2 * TPC * DK
    NG = max(1, TPC // 512)
    GW = min(512, TPC)
    NTG = max(1, HALF // 512)
    TGW = min(512, HALF)
    KCAP = 4 * CHUNK

    nc = bass.Bass("TRN2", target_bir_lowering=False)

    x_d = nc.dram_tensor("x", [TPC, D], F32, kind="ExternalInput")
    wq_d = nc.dram_tensor("wq", [D, DK], F32, kind="ExternalInput")
    wk_d = nc.dram_tensor("wk", [D, DK], F32, kind="ExternalInput")
    wv_d = nc.dram_tensor("wv", [D, DK], F32, kind="ExternalInput")
    wo_d = nc.dram_tensor("wo", [DK, D], F32, kind="ExternalInput")
    wg_d = nc.dram_tensor("wg", [D, E], F32, kind="ExternalInput")
    bg_d = nc.dram_tensor("bg", [E], F32, kind="ExternalInput")
    w1_d = nc.dram_tensor("w1", [E, D, H], F32, kind="ExternalInput")
    b1_d = nc.dram_tensor("b1", [E, H], F32, kind="ExternalInput")
    w2_d = nc.dram_tensor("w2", [E, H, D], F32, kind="ExternalInput")
    b2_d = nc.dram_tensor("b2", [E, D], F32, kind="ExternalInput")
    y_d = nc.dram_tensor("y", [TPC, D], F32, kind="ExternalOutput")

    kvb_d = nc.dram_tensor("kvb", [1, BLK], BF16)
    kvg_d = nc.dram_tensor("kvg", [NCORE, BLK], BF16)
    cntb_d = nc.dram_tensor("cntb", [1, 16], F32)
    cntg_d = nc.dram_tensor("cntg", [NCORE, 16], F32)

    big = contextlib.ExitStack()
    sb = lambda n, s, d: big.enter_context(nc.sbuf_tensor(n, s, d))
    sem_names = [
        "sINIT", "sW", "sLN1v", "sLN1s", "sSTAT", "sAFF", "sTRA", "sLNT",
        "sKP", "sK2", "sKNM", "sKNE", "sKE", "sQP", "sQE", "sVP", "sVE",
        "sKVB", "sCC", "sGKV", "sKC", "sMK", "sEMF", "sVS",
        "mmS", "expS", "avS", "maskS", "sEPI",
        "sSC", "sRC", "sWO", "sRES", "sSQ2", "sSTAT2", "sAFF2", "sTRA2", "sLNT2",
        "sGP", "sGPE", "sTOP", "sMT", "sMTE", "sCNTR", "sCNTB", "sCC2", "sG",
        "sWV", "sZF", "sTH", "sTHC", "sTHE", "sSEL", "sB1P", "sB1E", "sB1L",
        "sW2L", "sW1L", "sW1F", "sHP", "sGEL", "sSRP", "sSRE", "sMSK",
        "sWP", "sOUT", "sSTa", "sSTb", "sW1La", "sW1Lb",
    ] + ["sLDX%d" % i for i in range(TPC // 128)]
    S = {n: big.enter_context(nc.semaphore(n)) for n in sem_names}

    XBUF = sb("XBUF", [128, NTILE * D], F32)
    XT = sb("XT", [128, DC * TPC], BF16)
    RCOL = sb("RCOL", [128, NTILE], F32)
    STAT = sb("STAT", [128, 8 * NTILE], F32)
    IDN = sb("IDN", [128, 128], BF16)
    TRI = sb("TRI", [128, 128], BF16)
    ONES1 = sb("ONES1", [1, 128], BF16)
    ONE1F = sb("ONE1F", [1, 1], F32)
    ONE1B = sb("ONE1B", [1, 1], BF16)
    ONESC = sb("ONESC", [128, 1], BF16)
    OHE = sb("OHE", [8, E * 128], BF16)
    WO = sb("WO", [128, D], BF16)
    WG = sb("WG", [128, E * DC], BF16)
    BG = sb("BG", [1, E], BF16)
    B2 = sb("B2", [E, D], BF16)
    SELB = sb("SELB", [8, TPC], BF16)

    mid_scope = contextlib.ExitStack()
    sbm = lambda n, s, d: mid_scope.enter_context(nc.sbuf_tensor(n, s, d))
    XLT = sbm("XLT", [128, 2 * D], BF16)
    SCR32 = sbm("SCR32", [128, D], F32)
    OTn = sbm("OTn", [128, TPC], BF16)
    SROW = sbm("SROW", [1, TPC], F32)
    GATE = sbm("GATE", [128, E * NTILE], F32)
    MTM = sbm("MTM", [128, E * NTILE], BF16)
    T2A = sbm("T2A", [128, 1], F32)
    T2B = sbm("T2B", [128, E], F32)
    T2C = sbm("T2C", [128, E], F32)
    T2D = sbm("T2D", [128, 1], F32)
    MEM = sbm("MEM", [8, TPC], BF16)
    ZM = sbm("ZM", [8, CHUNK], F32)
    ZR = sbm("ZR", [8, CHUNK], F32)
    ZZ = sbm("ZZ", [8, CHUNK], F32)
    GCNT = sbm("GCNT", [8, 16], F32)
    CNT = sbm("CNT", [8, 2], F32)
    WVv = sbm("WVv", [8, 5], F32)
    RIDX = sbm("RIDX", [8, 1], F32)
    FROW = sbm("FROW", [1, 8], F32)
    T8 = sbm("T8", [8, 1], F32)
    THR = sbm("THR", [1, 16], F32)
    THC = sbm("THC", [8, 2], F32)

    attn_scope = contextlib.ExitStack()
    sba = lambda n, s, d: attn_scope.enter_context(nc.sbuf_tensor(n, s, d))
    WQ = sba("WQ", [128, D], BF16)
    WK = sba("WK", [128, D], BF16)
    WVW = sba("WVW", [128, D], BF16)
    QT = sba("QT", [128, TPC], BF16)
    KOWN = sba("KOWN", [128, TPC], BF16)
    VOWN = sba("VOWN", [128, TPC], BF16)
    KG = sba("KG", [128, T], BF16)
    VG = sba("VG", [128, T], BF16)
    MCOL = sba("MCOL", [128, KPG], F32)
    NEGM = sba("NEGM", [128, KPG], F32)
    EMF = sba("EMF", [128, KPG], F32)
    EMB = sba("EMB", [128, KPG], BF16)
    PT0 = sba("PT0", [128, W], BF16)
    PT1 = sba("PT1", [128, W], BF16)
    PTS = (PT0, PT1)

    ap = lambda t, off, dims: bass.AP(t, off, dims)
    ENG = ["sync", "scalar", "vector", "tensor", "gpsimd"]

    def mkP(active):
        return {e: _Eng(getattr(nc, e), e == active) for e in ENG}

    def corereg(active, want):
        if active in want:
            h = getattr(nc, active)
            return h.to_reg(nc.partition_id([h.engine]))
        return None

    p1 = contextlib.ExitStack()
    ps1 = lambda n, s, d: p1.enter_context(nc.psum_tensor(n, s, d))
    TP0 = ps1("TP0", [128, 1024], BF16)
    TP1 = ps1("TP1", [128, 1024], BF16)
    TPS = (TP0, TP1)
    PJ0 = ps1("PJ0", [128, 512], F32)
    PJ1 = ps1("PJ1", [128, 512], F32)
    PJS = (PJ0, PJ1)

    def emit_b1(active):
        P = mkP(active)
        C = dict.fromkeys(sem_names, 0)

        def inc(inst, sem, n=1):
            inst.then_inc(S[sem], n)
            C[sem] += n

        sy, sc, ve, te, gp = (P[e] for e in ENG)

        gp.memset(IDN[:, :], 0.0)
        gp.memset(TRI[:, :], 1.0)
        gp.memset(OHE[:, :], 1.0)
        gp.drain()
        gp.affine_select(OHE[:, :].rearrange("p (a b) -> p a b", b=128),
                         OHE[:, :].rearrange("p (a b) -> p a b", b=128),
                         [[-1, E], [0, 128]], ALU.is_equal, 0.0,
                         base=0, channel_multiplier=1)
        gp.affine_select(IDN[:, :], IDN[:, :], [[-1, 128]], ALU.not_equal, 1.0,
                         base=0, channel_multiplier=1)
        gp.affine_select(TRI[:, :], TRI[:, :], [[1, 128]], ALU.is_ge, 0.0,
                         base=0, channel_multiplier=-1)
        gp.memset(ONES1[:, :], 1.0)
        gp.memset(ONE1F[:, :], 1.0)
        gp.memset(ONE1B[:, :], 1.0)
        gp.memset(ONESC[:, :], 1.0)
        gp.memset(ZZ[:, :], 0.0)
        inc(gp.iota(RIDX[:, :], [[0, 1]], base=0, channel_multiplier=1,
                    allow_small_or_imprecise_dtypes=True), "sINIT")

        for t in range(NTILE):
            inc(sy.dma_start(out=XBUF[:, t * D:(t + 1) * D],
                             in_=x_d[t * 128:(t + 1) * 128, :]), "sLDX%d" % t, 16)

        for name, dst, srcd in (("wk", WK, wk_d), ("wv", WVW, wv_d), ("wq", WQ, wq_d)):
            inc(gp.dma_start(out=dst[:, :].rearrange("p (a b) -> p a b", b=DK),
                             in_=ap(srcd, 0, [[DK, 128], [128 * DK, DC], [1, DK]])), "sW", 16)
        inc(gp.dma_start(out=WO[:, :], in_=wo_d[:, :]), "sW", 16)
        for dc in range(DC):
            inc(gp.dma_start(out=WG[:, dc * E:(dc + 1) * E],
                             in_=ap(wg_d, dc * 128 * E, [[E, 128], [1, E]])), "sW", 16)
        inc(gp.dma_start(out=BG[:, :], in_=ap(bg_d, 0, [[E, 1], [1, E]])), "sW", 16)
        inc(gp.dma_start(out=B2[:, :], in_=b2_d[:, :]), "sW", 16)
        wAt = {"wk": C["sW"], "wv": C["sW"], "wq": C["sW"], "wo": C["sW"],
               "all": C["sW"]}

        for t in range(NTILE):
            xt = XBUF[:, t * D:(t + 1) * D]
            ve.wait_ge(S["sLDX%d" % t], 16)
            inc(ve.tensor_reduce(STAT[:, t:t + 1], xt, AX.X, ALU.add), "sLN1v")
            sc.wait_ge(S["sLDX%d" % t], 16)
            sc.activation(SCR32[:, :], xt, AF.Square,
                          accum_out=STAT[:, NTILE + t:NTILE + t + 1])
            inc(sc.drain(), "sLN1s")
        sxA = STAT[:, 0:NTILE]
        sx2A = STAT[:, NTILE:2 * NTILE]
        meanA = STAT[:, 2 * NTILE:3 * NTILE]
        varA = STAT[:, 3 * NTILE:4 * NTILE]
        rstdA = STAT[:, 4 * NTILE:5 * NTILE]
        negmrA = STAT[:, 5 * NTILE:6 * NTILE]
        tmpA = STAT[:, 6 * NTILE:7 * NTILE]
        ve.wait_ge(S["sLN1s"], NTILE)
        ve.drain()
        ve.tensor_scalar(meanA, sxA, 1.0 / D, None, ALU.mult)
        ve.drain()
        ve.tensor_scalar(varA, sx2A, 1.0 / D, None, ALU.mult)
        ve.drain()
        ve.tensor_tensor(tmpA, meanA, meanA, ALU.mult)
        ve.drain()
        ve.tensor_sub(varA, varA, tmpA)
        ve.drain()
        ve.tensor_scalar(varA, varA, EPS_LN, None, ALU.add)
        inc(ve.drain(), "sSTAT")
        sc.wait_ge(S["sSTAT"], 1)
        sc.activation(tmpA, varA, AF.Ln)
        sc.drain()
        sc.activation(rstdA, tmpA, AF.Exp, scale=-0.5)
        inc(sc.drain(), "sSTAT")
        ve.wait_ge(S["sSTAT"], 2)
        ve.tensor_mul(negmrA, meanA, rstdA)
        ve.drain()
        ve.tensor_scalar(negmrA, negmrA, -1.0, None, ALU.mult)
        inc(ve.drain(), "sSTAT")

        for t in range(NTILE):
            xt = XBUF[:, t * D:(t + 1) * D]
            sl = XLT[:, (t % 2) * D:((t % 2) + 1) * D]
            sc.wait_ge(S["sSTAT"], 3)
            if t >= 2:
                sc.wait_ge(S["sTRA"], t - 1)
            sc.activation(sl, xt, AF.Identity,
                          bias=negmrA[:, t:t + 1], scale=rstdA[:, t:t + 1])
            inc(sc.drain(), "sAFF")
            te.wait_ge(S["sAFF"], t + 1)
            te.wait_ge(S["sINIT"], 1)
            if t >= 2:
                te.wait_ge(S["sLNT"], t - 1)
            tp = TPS[t % 2]
            for g in range(DC):
                ins = te.transpose(tp[:, g * 128:(g + 1) * 128],
                                   sl[:, g * 128:(g + 1) * 128], IDN[:, :])
            inc(ins, "sTRA")
            ve.wait_ge(S["sTRA"], t + 1)
            dst = ap(XT, t * 128, [[DC * TPC, 128], [TPC, DC], [1, 128]])
            ve.tensor_copy(dst, tp[:, :].rearrange("p (a b) -> p a b", b=128))
            inc(ve.drain(), "sLNT")

        for g in range(NG):
            pj = PJS[g % 2]
            te.wait_ge(S["sLNT"], min(NTILE, ((g + 1) * GW + 127) // 128))
            te.wait_ge(S["sW"], wAt["all"])
            if g >= 2:
                te.wait_ge(S["sKE"], g - 1)
            for dc in range(DC):
                ins = te.matmul(pj[:, 0:GW], WK[:, dc * 128:(dc + 1) * 128],
                                XT[:, dc * TPC + g * GW: dc * TPC + (g + 1) * GW],
                                start=(dc == 0), stop=(dc == DC - 1))
            inc(ins, "sKP")
            ve.wait_ge(S["sKP"], g + 1)
            ve.tensor_copy(KOWN[:, g * GW:(g + 1) * GW], pj[:, 0:GW])
            inc(ve.drain(), "sKE")

        for t in range(NTILE):
            pj = PJS[t % 2]
            te.wait_ge(S["sLNT"], t + 1)
            te.wait_ge(S["sW"], wAt["all"])
            if t == 0:
                te.wait_ge(S["sKE"], NG)
            if t >= 2:
                te.wait_ge(S["sVE"], t - 1)
            for dc in range(DC):
                ins = te.matmul(pj[:, 0:128],
                                XT[:, dc * TPC + t * 128: dc * TPC + (t + 1) * 128],
                                WVW[:, dc * 128:(dc + 1) * 128],
                                start=(dc == 0), stop=(dc == DC - 1))
            inc(ins, "sVP")
            ve.wait_ge(S["sVP"], t + 1)
            ve.tensor_copy(VOWN[:, t * 128:(t + 1) * 128], pj[:, 0:128])
            inc(ve.drain(), "sVE")

        gp.wait_ge(S["sKE"], NG)
        gp.wait_ge(S["sVE"], NTILE)
        inc(gp.dma_start(out=ap(kvb_d, KOFF, [[TPC, 128], [1, TPC]]),
                         in_=KOWN[:, :]), "sKVB", 16)
        inc(gp.dma_start(out=ap(kvb_d, VOFF, [[DK, 128], [128 * DK, NTILE], [1, DK]]),
                         in_=VOWN[:, :].rearrange("p (a b) -> p a b", b=128)), "sKVB", 16)
        gp.wait_ge(S["sKVB"], 32)
        inc(gp.collective_compute(
            "AllGather", ALU.bypass, replica_groups=[list(range(NCORE))],
            ins=[kvb_d.ap().opt()], outs=[kvg_d.ap().opt()]), "sCC")

        for g in range(NG):
            pj = PJS[g % 2]
            te.wait_ge(S["sW"], wAt["all"])
            if g == 0:
                te.wait_ge(S["sVE"], NTILE)
            if g >= 2:
                te.wait_ge(S["sQE"], g - 1)
            for dc in range(DC):
                ins = te.matmul(pj[:, 0:GW], WQ[:, dc * 128:(dc + 1) * 128],
                                XT[:, dc * TPC + g * GW: dc * TPC + (g + 1) * GW],
                                start=(dc == 0), stop=(dc == DC - 1))
            inc(ins, "sQP")
            ve.wait_ge(S["sQP"], g + 1)
            ve.tensor_copy(QT[:, g * GW:(g + 1) * GW], pj[:, 0:GW])
            inc(ve.drain(), "sQE")

        for engname, which in (("sync", 0), ("scalar", 1)):
            eng = P[engname]
            reg = corereg(active, (engname,))
            for c in range(NCORE):
                with (eng.If_eq(reg, c) if reg is not None else _STUB):
                    if reg is None:
                        continue
                    r0 = 4 * (c // 4)
                    eng.wait_ge(S["sCC"], 1)
                    for r in range(4):
                        row = (r0 + r) * BLK
                        for l, ch in enumerate(_chunks_of_core(r0 + r)):
                            if which == 0:
                                eng.dma_start(
                                    out=KG[:, ch * CHUNK:(ch + 1) * CHUNK],
                                    in_=ap(kvg_d, row + KOFF + l * CHUNK,
                                           [[TPC, 128], [1, CHUNK]]),
                                ).then_inc(S["sGKV"], 16)
                            else:
                                eng.dma_start(
                                    out=VG[:, ch * CHUNK:(ch + 1) * CHUNK]
                                        .rearrange("p (a b) -> p a b", b=128),
                                    in_=ap(kvg_d, row + VOFF + l * CHUNK * DK,
                                           [[DK, 128], [128 * DK, KPC], [1, DK]]),
                                ).then_inc(S["sGKV"], 16)
            if reg is not None:
                getattr(nc, active).end_ifs()
        C["sGKV"] += 16 * 16
        return C

    with nc.Block() as block:
        block.sync(lambda e: emit_b1("sync"))
        block.scalar(lambda e: emit_b1("scalar"))
        block.vector(lambda e: emit_b1("vector"))
        block.tensor(lambda e: emit_b1("tensor"))
        block.gpsimd(lambda e: emit_b1("gpsimd"))
    CEND1 = emit_b1("__none__")
    p1.close()

    p2 = contextlib.ExitStack()
    ps2 = lambda n, s, d: p2.enter_context(nc.psum_tensor(n, s, d))
    SP0 = ps2("SP0", [128, W], F32)
    SP1 = ps2("SP1", [128, W], F32)
    SPS = (SP0, SP1)
    OTP = ps2("OTP", [128, W], F32)
    SSP = ps2("SSP", [1, W], F32)
    PIECES = _pieces(W)

    def emit_b2(active):
        P = mkP(active)
        C = dict(CEND1)

        def inc(inst, sem, n=1):
            inst.then_inc(S[sem], n)
            C[sem] += n

        sy, sc, ve, te, gp = (P[e] for e in ENG)

        NKGRP = max(1, KPG // KPC)
        for grp in range(NKGRP):
            ptb = PTS[grp % 2]
            sc.wait_ge(S["sGKV"], CEND1["sGKV"])
            if grp >= 2:
                sc.wait_ge(S["sKC"], grp - 1)
            inc(sc.activation(ptb[:, 0:KPC * 128],
                              KG[:, grp * KPC * 128:(grp + 1) * KPC * 128],
                              AF.Square), "sK2")
            te.wait_ge(S["sK2"], grp + 1)
            te.wait_ge(S["sINIT"], 1)
            for j in range(KPC):
                kt = grp * KPC + j
                ins = te.matmul(SP0[:, kt:kt + 1], ptb[:, j * 128:(j + 1) * 128],
                                ONESC[:, :], start=True, stop=True)
            inc(ins, "sKC")
        sc.wait_ge(S["sKC"], C["sKC"])
        sc.drain()
        sc.activation(MCOL[:, :], SP0[:, 0:KPG], AF.Sqrt, scale=QB * QB / DK)
        inc(sc.drain(), "sMK")
        ve.wait_ge(S["sMK"], C["sMK"])
        ve.tensor_scalar(NEGM[:, :], MCOL[:, :], -1.0, None, ALU.mult)
        ve.drain()
        sc.wait_ge(S["sMK"], C["sMK"])
        sc.activation(EMB[:, :], MCOL[:, :], AF.Exp)
        inc(sc.drain(), "sMK")
        ve.wait_ge(S["sMK"], C["sMK"])
        ve.tensor_copy(EMF[:, :], EMB[:, :])
        inc(ve.drain(), "sEMF")
        ve.wait_ge(S["sGKV"], CEND1["sGKV"])
        for kt in range(KPG):
            vs = VG[:, kt * 128:(kt + 1) * 128]
            ins = ve.tensor_scalar(vs, vs, EMF[:, kt:kt + 1], None, ALU.mult)
        inc(ve.drain(), "sVS")

        B = dict(C)

        def sched(c):
            out = []
            for qidx, ch in enumerate(_chunks_of_core(c)):
                nk = KPC * (ch + 1)
                for kt in range(nk):
                    out.append((qidx, ch, kt, nk))
            return out

        def ndiag_upto(sch, ii):
            return sum(1 for (q2, c2, k2, n2) in sch[:ii + 1] if k2 >= n2 - KPC)

        def emit_av(sch, ii):
            qidx, ch, kt, nk = sch[ii]
            diag = kt >= nk - KPC
            pt = PTS[ii % 2]
            first, last = kt == 0, kt == nk - 1
            if ii == 0:
                te.wait_ge(S["sVS"], B["sVS"])
            if diag:
                te.wait_ge(S["maskS"], B["maskS"] + ndiag_upto(sch, ii))
            else:
                te.wait_ge(S["expS"], B["expS"] + ii + 1)
            if first and qidx == 1:
                te.wait_ge(S["sEPI"], B["sEPI"] + 1)
            for o, wdt in PIECES:
                te.matmul(OTP[:, o:o + wdt], VG[:, kt * 128:(kt + 1) * 128],
                          pt[:, o:o + wdt], start=first, stop=last)
            for o, wdt in PIECES:
                ins = te.matmul(SSP[0:1, o:o + wdt], EMB[:, kt:kt + 1],
                                pt[:, o:o + wdt], start=first, stop=last)
            ins.then_inc(S["avS"], 1)

        def attn_body(c, engname):
            sch = sched(c)
            NIT = len(sch)
            nkA = KPC * (_chunks_of_core(c)[0] + 1)
            for i, (qidx, ch, kt, nk) in enumerate(sch):
                diag = kt >= nk - KPC
                j = kt - (nk - KPC)
                qoff = qidx * W
                sp = SPS[i % 2]
                pt = PTS[i % 2]
                if engname == "tensor":
                    if i >= 2:
                        te.wait_ge(S["expS"], B["expS"] + i - 1)
                    if i == 0:
                        te.wait_ge(S["sEMF"], B["sEMF"])
                        te.wait_ge(S["sQE"], CEND1["sQE"])
                    for pi, (o, wdt) in enumerate(PIECES):
                        ins = te.matmul(sp[:, o:o + wdt], KG[:, kt * 128:(kt + 1) * 128],
                                        QT[:, qoff + o:qoff + o + wdt],
                                        start=(pi == 0), stop=(pi == len(PIECES) - 1))
                    ins.then_inc(S["mmS"], 1)
                    if i >= 1:
                        emit_av(sch, i - 1)
                elif engname == "scalar":
                    sc.wait_ge(S["mmS"], B["mmS"] + i + 1)
                    if i == 0:
                        sc.wait_ge(S["sEMF"], B["sEMF"])
                    if i >= 2:
                        sc.wait_ge(S["avS"], B["avS"] + i - 1)
                    sc.activation(pt[:, :], sp[:, :], AF.Exp,
                                  bias=NEGM[:, kt:kt + 1], scale=SCALE)
                    sc.drain().then_inc(S["expS"], 1)
                elif engname == "vector":
                    if diag:
                        ve.wait_ge(S["expS"], B["expS"] + i + 1)
                        ve.wait_ge(S["sINIT"], 1)
                        if j > 0:
                            ve.memset(pt[:, 0:j * 128], 0.0)
                        ve.tensor_mul(pt[:, j * 128:(j + 1) * 128],
                                      pt[:, j * 128:(j + 1) * 128],
                                      TRI[:, :])
                        ve.drain().then_inc(S["maskS"], 1)
                    if i == nkA - 1 or i == NIT - 1:
                        qx = 0 if i == nkA - 1 else 1
                        upto = nkA if qx == 0 else NIT
                        ve.wait_ge(S["avS"], B["avS"] + upto)
                        ve.tensor_copy(SROW[0:1, qx * W:(qx + 1) * W], SSP[0:1, :])
                        ve.tensor_copy(OTn[:, qx * W:(qx + 1) * W], OTP[:, :])
                        ve.drain().then_inc(S["sEPI"], 1)
            if engname == "tensor":
                emit_av(sch, NIT - 1)

        for engname in ("tensor", "scalar", "vector"):
            reg = corereg(active, (engname,)) if active == engname else None
            for c in range(NCORE):
                with (P[engname].If_eq(reg, c) if reg is not None else _STUB):
                    if reg is not None:
                        attn_body(c, engname)
            if reg is not None:
                getattr(nc, active).end_ifs()
        NIT = 9 * KPC
        C["mmS"] += NIT
        C["expS"] += NIT
        C["avS"] += NIT
        C["maskS"] += 2 * KPC
        C["sEPI"] += 2
        return C

    with nc.Block() as block:
        block.sync(lambda e: emit_b2("sync"))
        block.scalar(lambda e: emit_b2("scalar"))
        block.vector(lambda e: emit_b2("vector"))
        block.tensor(lambda e: emit_b2("tensor"))
        block.gpsimd(lambda e: emit_b2("gpsimd"))
    CEND2 = emit_b2("__none__")
    p2.close()
    attn_scope.close()

    p3 = contextlib.ExitStack()
    ps3 = lambda n, s, d: p3.enter_context(nc.psum_tensor(n, s, d))
    WOP0 = ps3("WOP0", [128, D], F32)
    WOP1 = ps3("WOP1", [128, D], F32)
    WOPS = (WOP0, WOP1)
    TP0b = ps3("TP0b", [128, 1024], BF16)
    TP1b = ps3("TP1b", [128, 1024], BF16)
    TPSb = (TP0b, TP1b)
    MISA = ps3("MISA", [128, 512], F32)
    MISB = ps3("MISB", [128, 512], F32)

    def emit_b3(active):
        P = mkP(active)
        C = dict(CEND2)

        def inc(inst, sem, n=1):
            inst.then_inc(S[sem], n)
            C[sem] += n

        sy, sc, ve, te, gp = (P[e] for e in ENG)

        SCP = MISA[:, 0:NTILE]
        ZP0 = MISA[0:1, 32:32 + E]
        ZP1 = MISA[0:1, 40:40 + E]
        FPp = MISA[0:1, 48:48 + E]
        THCp = MISA[0:8, 64:66]
        GP = MISB[:, 0:E]

        te.wait_ge(S["sEPI"], CEND2["sEPI"])
        te.wait_ge(S["sINIT"], 1)
        for t in range(NTILE):
            ins = te.matmul(SCP[:, t:t + 1], SROW[0:1, t * 128:(t + 1) * 128],
                            ONE1F[:, :], start=True, stop=True)
        inc(ins, "sSC")
        sc.wait_ge(S["sSC"], C["sSC"])
        sc.activation(STAT[:, 6 * NTILE:7 * NTILE], SCP, AF.Ln)
        sc.drain()
        sc.activation(RCOL[:, :], STAT[:, 6 * NTILE:7 * NTILE], AF.Exp, scale=-1.0)
        inc(sc.drain(), "sRC")

        bWO, bRES, bSQ2 = C["sWO"], C["sRES"], C["sSQ2"]
        for t in range(NTILE):
            wop = WOPS[t % 2]
            te.wait_ge(S["sW"], CEND1["sW"])
            if t >= 2:
                te.wait_ge(S["sRES"], bRES + t - 1)
            for o, wdt in _pieces(D):
                ins = te.matmul(wop[:, o:o + wdt], OTn[:, t * 128:(t + 1) * 128],
                                WO[:, o:o + wdt], start=True, stop=True)
            inc(ins, "sWO")
            xt = XBUF[:, t * D:(t + 1) * D]
            ve.wait_ge(S["sWO"], bWO + t + 1)
            ve.wait_ge(S["sRC"], C["sRC"])
            ve.scalar_tensor_tensor(xt, wop[:, :], RCOL[:, t:t + 1], xt,
                                    ALU.mult, ALU.add,
                                    accum_out=STAT[:, t:t + 1])
            inc(ve.drain(), "sRES")
            sc.wait_ge(S["sRES"], bRES + t + 1)
            sc.activation(SCR32[:, :], xt, AF.Square,
                          accum_out=STAT[:, NTILE + t:NTILE + t + 1])
            inc(sc.drain(), "sSQ2")

        sxA = STAT[:, 0:NTILE]
        sx2A = STAT[:, NTILE:2 * NTILE]
        meanA = STAT[:, 2 * NTILE:3 * NTILE]
        varA = STAT[:, 3 * NTILE:4 * NTILE]
        rstdA = STAT[:, 4 * NTILE:5 * NTILE]
        negmrA = STAT[:, 5 * NTILE:6 * NTILE]
        tmpA = STAT[:, 6 * NTILE:7 * NTILE]
        bS2 = C["sSTAT2"]
        ve.wait_ge(S["sSQ2"], bSQ2 + NTILE)
        ve.drain()
        ve.tensor_scalar(meanA, sxA, 1.0 / D, None, ALU.mult)
        ve.drain()
        ve.tensor_scalar(varA, sx2A, 1.0 / D, None, ALU.mult)
        ve.drain()
        ve.tensor_tensor(tmpA, meanA, meanA, ALU.mult)
        ve.drain()
        ve.tensor_sub(varA, varA, tmpA)
        ve.drain()
        ve.tensor_scalar(varA, varA, EPS_LN, None, ALU.add)
        inc(ve.drain(), "sSTAT2")
        sc.wait_ge(S["sSTAT2"], bS2 + 1)
        sc.activation(tmpA, varA, AF.Ln)
        sc.drain()
        sc.activation(rstdA, tmpA, AF.Exp, scale=-0.5)
        inc(sc.drain(), "sSTAT2")
        ve.wait_ge(S["sSTAT2"], bS2 + 2)
        ve.tensor_mul(negmrA, meanA, rstdA)
        ve.drain()
        ve.tensor_scalar(negmrA, negmrA, -1.0, None, ALU.mult)
        inc(ve.drain(), "sSTAT2")

        bAF, bTR, bLT = C["sAFF2"], C["sTRA2"], C["sLNT2"]
        for t in range(NTILE):
            xt = XBUF[:, t * D:(t + 1) * D]
            sl = XLT[:, (t % 2) * D:((t % 2) + 1) * D]
            sc.wait_ge(S["sSTAT2"], bS2 + 3)
            if t >= 2:
                sc.wait_ge(S["sTRA2"], bTR + t - 1)
            sc.activation(sl, xt, AF.Identity,
                          bias=negmrA[:, t:t + 1], scale=rstdA[:, t:t + 1])
            inc(sc.drain(), "sAFF2")
            te.wait_ge(S["sAFF2"], bAF + t + 1)
            if t >= 2:
                te.wait_ge(S["sLNT2"], bLT + t - 1)
            tp = TPSb[t % 2]
            for g in range(DC):
                ins = te.transpose(tp[:, g * 128:(g + 1) * 128],
                                   sl[:, g * 128:(g + 1) * 128], IDN[:, :])
            inc(ins, "sTRA2")
            ve.wait_ge(S["sTRA2"], bTR + t + 1)
            dst = ap(XT, t * 128, [[DC * TPC, 128], [TPC, DC], [1, 128]])
            ve.tensor_copy(dst, tp[:, :].rearrange("p (a b) -> p a b", b=128))
            inc(ve.drain(), "sLNT2")

        bGP, bGPE = C["sGP"], C["sGPE"]
        for t in range(NTILE):
            te.wait_ge(S["sLNT2"], bLT + t + 1)
            te.wait_ge(S["sW"], CEND1["sW"])
            te.wait_ge(S["sRC"], C["sRC"])
            if t > 0:
                te.wait_ge(S["sGPE"], bGPE + t)
            for dc in range(DC):
                te.matmul(GP, XT[:, dc * TPC + t * 128: dc * TPC + (t + 1) * 128],
                          WG[:, dc * E:(dc + 1) * E], start=(dc == 0), stop=False)
            inc(te.matmul(GP, ONES1[:, :], BG[:, :], start=False, stop=True), "sGP")
            ve.wait_ge(S["sGP"], bGP + t + 1)
            ve.tensor_copy(GATE[:, t * E:(t + 1) * E], GP)
            inc(ve.drain(), "sGPE")

        bTOP = C["sTOP"]
        for t in range(NTILE):
            g_t = GATE[:, t * E:(t + 1) * E]
            ve.wait_ge(S["sGPE"], bGPE + t + 1)
            ve.drain()
            ve.tensor_reduce(T2A[:, :], g_t, AX.X, ALU.max)
            ve.drain()
            ve.tensor_scalar(T2B[:, :], g_t, T2A[:, 0:1], None, ALU.is_equal)
            ve.drain()
            ve.scalar_tensor_tensor(T2C[:, :], T2B[:, :], -1e30, g_t, ALU.mult, ALU.add)
            ve.drain()
            ve.tensor_reduce(T2D[:, :], T2C[:, :], AX.X, ALU.max)
            ve.drain()
            ve.tensor_scalar(T2C[:, :], T2C[:, :], T2D[:, 0:1], None, ALU.is_equal)
            ve.drain()
            ve.tensor_reduce(T2D[:, :], T2B[:, :], AX.X, ALU.add)
            ve.drain()
            ve.tensor_scalar(T2D[:, :], T2D[:, :], -1.0, 2.0, ALU.mult, ALU.add)
            ve.drain()
            ve.tensor_scalar(T2D[:, :], T2D[:, :], 0.0, None, ALU.max)
            ve.drain()
            ve.tensor_scalar(T2C[:, :], T2C[:, :], T2D[:, 0:1], None, ALU.mult)
            ve.drain()
            ve.tensor_add(MTM[:, t * E:(t + 1) * E], T2B[:, :], T2C[:, :])
            inc(ve.drain(), "sTOP")

        bMT, bMTE = C["sMT"], C["sMTE"]
        for t in range(NTILE):
            tp = TPSb[t % 2]
            te.wait_ge(S["sTOP"], bTOP + t + 1)
            te.wait_ge(S["sLNT2"], bLT + NTILE)
            if t >= 2:
                te.wait_ge(S["sMTE"], bMTE + t - 1)
            inc(te.transpose(tp[0:E, 0:128], MTM[:, t * E:(t + 1) * E], IDN[:, :]), "sMT")
            ve.wait_ge(S["sMT"], bMT + t + 1)
            ve.tensor_copy(MEM[0:E, t * 128:(t + 1) * 128], tp[0:E, 0:128])
            inc(ve.drain(), "sMTE")

        ve.wait_ge(S["sMTE"], bMTE + NTILE)
        ve.drain()
        for l in range(2):
            ins = ve.tensor_reduce(CNT[0:E, l:l + 1], MEM[0:E, l * CHUNK:(l + 1) * CHUNK],
                                   AX.X, ALU.add)
        inc(ve.drain(), "sCNTR")
        sy.wait_ge(S["sCNTR"], C["sCNTR"])
        inc(sy.dma_start(out=ap(cntb_d, 0, [[1, E], [E, 2]]), in_=CNT[0:E, :],
                 allow_slow_non_contiguous=True), "sCNTB", 16)
        gp.wait_ge(S["sCNTB"], C["sCNTB"])
        inc(gp.collective_compute(
            "AllGather", ALU.bypass, replica_groups=[list(range(NCORE))],
            ins=[cntb_d.ap().opt()], outs=[cntg_d.ap().opt()]), "sCC2")
        sy.wait_ge(S["sCC2"], C["sCC2"])
        inc(sy.dma_start(out=GCNT[0:E, :], in_=cntg_d[:, :]), "sG", 16)

        bWV, bZF, bTH = C["sWV"], C["sZF"], C["sTH"]

        def fill_body(c, engname):
            cA, cB = _chunks_of_core(c)
            r0 = 4 * (c // 4)
            if engname == "vector":
                ve.wait_ge(S["sINIT"], 1)
                specs = []
                for col, k in ((0, cA), (2, cB)):
                    specs.append((col, r0, r0 + min(4, k)))
                    specs.append((col + 1, r0 + max(0, 8 - k), r0 + 4))
                specs.append((4, r0, r0 + 4))
                for col, lo, hi in specs:
                    ve.tensor_scalar(WVv[:, col:col + 1], RIDX[:, :],
                                     float(lo) - 0.5, None, ALU.is_gt)
                    ve.tensor_scalar(T8[:, :], RIDX[:, :],
                                     float(hi) - 0.5, None, ALU.is_lt)
                    ve.drain()
                    ve.tensor_mul(WVv[:, col:col + 1], WVv[:, col:col + 1], T8[:, :])
                    ve.drain()
                ve.drain().then_inc(S["sWV"], 1)
                ve.wait_ge(S["sZF"], bZF + 1)
                ve.tensor_copy(FROW[:, :], FPp)
                ve.drain()
                for idx, k in enumerate((cA, cB)):
                    zp = (ZP0, ZP1)[idx]
                    th = THR[0:1, idx * 8:idx * 8 + E]
                    ve.tensor_sub(th, zp, FROW[:, :])
                    ve.drain()
                    ve.tensor_scalar(th, th, float(KCAP - CHUNK * k), None, ALU.add)
                    ve.drain()
                ve.drain().then_inc(S["sTH"], 1)
            else:
                te.wait_ge(S["sWV"], bWV + 1)
                te.wait_ge(S["sG"], C["sG"])
                te.wait_ge(S["sRC"], C["sRC"])
                for idx in range(2):
                    zp = (ZP0, ZP1)[idx]
                    col = idx * 2
                    te.matmul(zp, WVv[0:8, col:col + 1], GCNT[0:8, 0:E],
                              start=True, stop=False)
                    te.matmul(zp, WVv[0:8, col + 1:col + 2], GCNT[0:8, E:2 * E],
                              start=False, stop=True)
                te.matmul(FPp, WVv[0:8, 4:5], GCNT[0:8, 0:E], start=True, stop=False)
                te.matmul(FPp, WVv[0:8, 4:5], GCNT[0:8, E:2 * E],
                          start=False, stop=True).then_inc(S["sZF"], 1)

        for engname in ("tensor", "vector"):
            reg = corereg(active, (engname,)) if active == engname else None
            for c in range(NCORE):
                with (P[engname].If_eq(reg, c) if reg is not None else _STUB):
                    if reg is not None:
                        fill_body(c, engname)
            if reg is not None:
                getattr(nc, active).end_ifs()
        C["sWV"] += 1
        C["sZF"] += 1
        C["sTH"] += 1

        te.wait_ge(S["sTH"], C["sTH"])
        for idx in range(2):
            ins = te.matmul(THCp[:, idx:idx + 1], THR[0:1, idx * 8:idx * 8 + E],
                            ONE1F[:, :], start=True, stop=True)
        inc(ins, "sTHC")
        ve.wait_ge(S["sTHC"], C["sTHC"])
        ve.tensor_copy(THC[:, :], THCp)
        inc(ve.drain(), "sTHE")
        for l in range(2):
            mslice = MEM[0:E, l * CHUNK:(l + 1) * CHUNK]
            sslice = SELB[0:E, l * CHUNK:(l + 1) * CHUNK]
            ve.wait_ge(S["sTHE"], C["sTHE"])
            ve.drain()
            ve.tensor_scalar(ZM[0:E, :], mslice, -1.0, 1.0, ALU.mult, ALU.add)
            ve.drain()
            ve.tensor_tensor_scan(ZR[0:E, :], ZM[0:E, :], ZZ[0:E, :], 0.0, ALU.add, ALU.add)
            ve.drain()
            ve.tensor_scalar(ZR[0:E, :], ZR[0:E, :], THC[0:E, l:l + 1], None, ALU.is_le)
            ve.drain()
            ve.tensor_mul(ZM[0:E, :], ZM[0:E, :], ZR[0:E, :])
            ve.drain()
            ins = ve.scalar_tensor_tensor(sslice, mslice, 1.0, ZM[0:E, :],
                                          ALU.mult, ALU.add)
        inc(ve.drain(), "sSEL")

        return C

    with nc.Block() as block:
        block.sync(lambda e: emit_b3("sync"))
        block.scalar(lambda e: emit_b3("scalar"))
        block.vector(lambda e: emit_b3("vector"))
        block.tensor(lambda e: emit_b3("tensor"))
        block.gpsimd(lambda e: emit_b3("gpsimd"))
    CEND3 = emit_b3("__none__")
    p3.close()
    mid_scope.close()

    ffn_scope = contextlib.ExitStack()
    sbf = lambda n, s, d: ffn_scope.enter_context(nc.sbuf_tensor(n, s, d))
    B1F = sbf("B1F", [1, E * H], BF16)
    B1C = sbf("B1C", [128, 2 * E], F32)
    OUTT = sbf("OUTT", [128, 2 * D], F32)
    W2A = sbf("W2A", [128, E * HC * D], BF16)
    W1B0 = sbf("W1B0", [128, D * HC], BF16)
    W1B1 = sbf("W1B1", [128, D * HC], BF16)
    W1BS = (W1B0, W1B1)
    HSB = sbf("HSB", [128, E * HC * HALF], BF16)
    SREP0 = sbf("SREP0", [128, HALF], BF16)
    SREP1 = sbf("SREP1", [128, HALF], BF16)
    SREPS = (SREP0, SREP1)

    p4 = contextlib.ExitStack()
    ps4 = lambda n, s, d: p4.enter_context(nc.psum_tensor(n, s, d))
    HPS = [ps4(f"HP{i}", [128, TGW], F32) for i in range(HC * NTG)]
    WP0 = ps4("WP0", [128, D], F32)
    WP1 = ps4("WP1", [128, D], F32)
    WPS = (WP0, WP1)

    def emit_b4(active):
        P = mkP(active)
        C = dict(CEND3)

        def inc(inst, sem, n=1):
            inst.then_inc(S[sem], n)
            C[sem] += n

        sy, sc, ve, te, gp = (P[e] for e in ENG)
        bW2L = C["sW2L"]
        bHP, bGEL, bSRP, bSRE, bMSK = C["sHP"], C["sGEL"], C["sSRP"], C["sSRE"], C["sMSK"]
        bWP, bOUT = C["sWP"], C["sOUT"]

        gp.wait_ge(S["sSEL"], CEND3["sSEL"])
        inc(gp.dma_start(out=B1F[:, :], in_=ap(b1_d, 0, [[E * H, 1], [1, E * H]])),
            "sB1L", 16)
        te.wait_ge(S["sB1L"], C["sB1L"])
        te.wait_ge(S["sINIT"], 1)
        B1P = WP1[:, 0:2 * E]
        for e in range(E):
            for hc in range(HC):
                ins = te.matmul(B1P[:, e * HC + hc:e * HC + hc + 1],
                                B1F[0:1, e * H + hc * 128:e * H + (hc + 1) * 128],
                                ONE1B[:, :], start=True, stop=True)
        inc(ins, "sB1P")
        ve.wait_ge(S["sB1P"], C["sB1P"])
        ve.tensor_copy(B1C[:, :], B1P)
        inc(ve.drain(), "sB1E")

        gp.wait_ge(S["sEPI"], CEND2["sEPI"])
        for e in range(E):
            for hc in range(HC):
                inc(gp.dma_start(out=W2A[:, (e * HC + hc) * D:(e * HC + hc + 1) * D],
                                 in_=w2_d[e, hc * 128:(hc + 1) * 128, :]), "sW2L", 16)

        for half in range(2):
            for e in range(E):
                seq = half * E + e
                if seq >= 2:
                    gp.wait_ge(S["sHP"], bHP + HC * NTG * (seq - 1))
                inc(gp.dma_start(
                    out=W1BS[e % 2][:, :].rearrange("p (a b) -> p a b", b=H),
                    in_=w1_d[e, :, :].rearrange("(a p) b -> p a b", p=128)),
                    "sW1La" if e % 2 == 0 else "sW1Lb", 16)

            for e in range(E):
                seq = half * E + e
                buf = W1BS[e % 2]
                te.wait_ge(S["sSEL"], CEND3["sSEL"])
                if seq >= 1:
                    te.wait_ge(S["sSRE"], bSRE + seq)
                if half == 1 and e == 0:
                    te.wait_ge(S["sOUT"], bOUT + NTILE_H)
                for o, wdt in _pieces(HALF):
                    ins = te.matmul(WP0[:, o:o + wdt], OHE[0:8, e * 128:(e + 1) * 128],
                                    SELB[0:8, half * HALF + o:half * HALF + o + wdt],
                                    start=True, stop=True)
                inc(ins, "sSRP")
                ve.wait_ge(S["sSRP"], bSRP + seq + 1)
                ve.tensor_copy(SREPS[e % 2][:, :], WP0[:, 0:HALF])
                inc(ve.drain(), "sSRE")

                te.wait_ge(S["sW1La" if e % 2 == 0 else "sW1Lb"],
                           16 * (seq // 2 + 1))
                if seq == 0:
                    te.wait_ge(S["sLNT2"], CEND3["sLNT2"])
                for hc in range(HC):
                    for tg in range(NTG):
                        hp = HPS[hc * NTG + tg]
                        if seq >= 1:
                            te.wait_ge(S["sGEL"],
                                       bGEL + (seq - 1) * HC * NTG + hc * NTG + tg + 1)
                        for dc in range(DC):
                            ins = te.matmul(
                                hp[:, :],
                                buf[:, (dc * HC + hc) * 128:(dc * HC + hc + 1) * 128],
                                XT[:, dc * TPC + half * HALF + tg * TGW:
                                   dc * TPC + half * HALF + (tg + 1) * TGW],
                                start=(dc == 0), stop=(dc == DC - 1))
                        inc(ins, "sHP")
                for hc in range(HC):
                    for tg in range(NTG):
                        hp = HPS[hc * NTG + tg]
                        sc.wait_ge(S["sHP"], bHP + seq * HC * NTG + hc * NTG + tg + 1)
                        if seq == 0 and hc == 0 and tg == 0:
                            sc.wait_ge(S["sB1E"], C["sB1E"])
                        if half == 1 and e == 0 and hc == 0 and tg == 0:
                            sc.wait_ge(S["sWP"], bWP + NTILE_H)
                        dst = HSB[:, (e * HC + hc) * HALF + tg * TGW:
                                  (e * HC + hc) * HALF + (tg + 1) * TGW]
                        sc.activation(dst, hp[:, :], AF.Gelu,
                                      bias=B1C[:, e * HC + hc:e * HC + hc + 1])
                        inc(sc.drain(), "sGEL")
                for hc in range(HC):
                    h_sl = HSB[:, (e * HC + hc) * HALF:(e * HC + hc + 1) * HALF]
                    ve.wait_ge(S["sGEL"], bGEL + seq * HC * NTG + (hc + 1) * NTG)
                    ve.wait_ge(S["sSRE"], bSRE + seq + 1)
                    ins = ve.tensor_mul(h_sl, h_sl, SREPS[e % 2][:, :])
                inc(ve.drain(), "sMSK")

            for tt in range(NTILE_H):
                wp = WPS[tt % 2]
                gt = half * NTILE_H + tt
                if tt % 2 == 1 and gt <= 1:
                    te.wait_ge(S["sB1E"], C["sB1E"])
                te.wait_ge(S["sMSK"], bMSK + (half + 1) * E)
                te.wait_ge(S["sW2L"], bW2L + 16 * E * HC)
                te.wait_ge(S["sW"], CEND1["sW"])
                if gt >= 2:
                    te.wait_ge(S["sOUT"], bOUT + gt - 1)
                for e in range(E):
                    for hc in range(HC):
                        lhs = HSB[:, (e * HC + hc) * HALF + tt * 128:
                                  (e * HC + hc) * HALF + (tt + 1) * 128]
                        w2sl = W2A[:, (e * HC + hc) * D:(e * HC + hc + 1) * D]
                        for o, wdt in _pieces(D):
                            te.matmul(wp[:, o:o + wdt], lhs, w2sl[:, o:o + wdt],
                                      start=(e == 0 and hc == 0), stop=False)
                selt = SELB[0:E, half * HALF + tt * 128:half * HALF + (tt + 1) * 128]
                for o, wdt in _pieces(D):
                    ins = te.matmul(wp[:, o:o + wdt], selt, B2[0:E, o:o + wdt],
                                    start=False, stop=True)
                inc(ins, "sWP")
                xt = XBUF[:, gt * D:(gt + 1) * D]
                ot = OUTT[:, (gt % 2) * D:((gt % 2) + 1) * D]
                ve.wait_ge(S["sWP"], bWP + gt + 1)
                if gt >= 2:
                    ve.wait_ge(S["sSTa" if gt % 2 == 0 else "sSTb"],
                               16 * ((gt - 2) // 2 + 1))
                ve.tensor_add(ot, xt, wp[:, :])
                inc(ve.drain(), "sOUT")
                sy.wait_ge(S["sOUT"], bOUT + gt + 1)
                inc(sy.dma_start(out=y_d[gt * 128:(gt + 1) * 128, :], in_=ot),
                    "sSTa" if gt % 2 == 0 else "sSTb", 16)

        sy.wait_ge(S["sSTa"], 16 * NTILE_H)
        sy.wait_ge(S["sSTb"], 16 * NTILE_H)
        return C

    with nc.Block() as block:
        block.sync(lambda e: emit_b4("sync"))
        block.scalar(lambda e: emit_b4("scalar"))
        block.vector(lambda e: emit_b4("vector"))
        block.tensor(lambda e: emit_b4("tensor"))
        block.gpsimd(lambda e: emit_b4("gpsimd"))
    p4.close()
    ffn_scope.close()
    big.close()
    return nc


def kernel(L0_attn_x, L0_attn_Wq, L0_attn_Wk, L0_attn_Wv, L0_attn_Wo,
           L0_ffn_w_gate_w, L0_ffn_w_gate_b, L0_ffn_W1, L0_ffn_b1,
           L0_ffn_W2, L0_ffn_b2):
    x = np.ascontiguousarray(np.asarray(L0_attn_x, dtype=np.float32))
    B, T, Dm = x.shape
    CHUNK = T // NCHUNK
    nc = build_nc(T)
    shared = {
        "wq": np.ascontiguousarray(np.asarray(L0_attn_Wq, np.float32)),
        "wk": np.ascontiguousarray(np.asarray(L0_attn_Wk, np.float32)),
        "wv": np.ascontiguousarray(np.asarray(L0_attn_Wv, np.float32)),
        "wo": np.ascontiguousarray(np.asarray(L0_attn_Wo, np.float32)),
        "wg": np.ascontiguousarray(np.asarray(L0_ffn_w_gate_w, np.float32)),
        "bg": np.ascontiguousarray(np.asarray(L0_ffn_w_gate_b, np.float32)),
        "w1": np.ascontiguousarray(np.asarray(L0_ffn_W1, np.float32)),
        "b1": np.ascontiguousarray(np.asarray(L0_ffn_b1, np.float32)),
        "w2": np.ascontiguousarray(np.asarray(L0_ffn_W2, np.float32)),
        "b2": np.ascontiguousarray(np.asarray(L0_ffn_b2, np.float32)),
    }
    in_maps = []
    for c in range(NCORE):
        b = c // 4
        cA, cB = _chunks_of_core(c)
        xs = np.concatenate([x[b, cA * CHUNK:(cA + 1) * CHUNK],
                             x[b, cB * CHUNK:(cB + 1) * CHUNK]], axis=0)
        in_maps.append({"x": np.ascontiguousarray(xs), **shared})
    res = run_bass_kernel_spmd(nc, in_maps, core_ids=list(range(NCORE)))
    out = np.empty((B, T, Dm), dtype=np.float32)
    for c in range(NCORE):
        b = c // 4
        cA, cB = _chunks_of_core(c)
        yc = np.asarray(res.results[c]["y"], dtype=np.float32)
        out[b, cA * CHUNK:(cA + 1) * CHUNK] = yc[0:CHUNK]
        out[b, cB * CHUNK:(cB + 1) * CHUNK] = yc[CHUNK:2 * CHUNK]
    return out
